# version 02
# baseline (speedup 1.0000x reference)
import numpy as np
import ml_dtypes
from contextlib import ExitStack

import concourse.bass as bass
import concourse.mybir as mybir
from concourse.bass_utils import run_bass_kernel_spmd

F32 = mybir.dt.float32
BF16 = mybir.dt.bfloat16
I32 = mybir.dt.int32
AF = mybir.ActivationFunctionType
ALU = mybir.AluOpType
AX = mybir.AxisListType

B, T, E = 32, 2048, 1024
H, DH, CH = 16, 64, 512
NCH = T // CH
NTB = CH // 128
NET = E // 128
NFT = H // 2
NCORES = 8
BPC = B // NCORES

MAGIC = 1597463007.0


def build_program(nb=BPC, dbg=False):
    nc = bass.Bass("TRN2", target_bir_lowering=False, dynamic_dma_scratch_size=512)

    dr = lambda n, s, k: nc.dram_tensor(n, s, F32, kind=k).ap()
    x_h = dr("x", [nb, T, E], "ExternalInput")
    wqkvT_h = dr("wqkvT", [E, 3 * E], "ExternalInput")
    woutT_h = dr("woutT", [E, E], "ExternalInput")
    wffnT_h = dr("wffnT", [E, E], "ExternalInput")
    biasrep_h = dr("biasrep", [128, E], "ExternalInput")
    alpharep_h = dr("alpharep", [128, 1], "ExternalInput")
    abdt_h = dr("abdt", [16, 3], "ExternalInput")
    drb = lambda n, s: nc.dram_tensor(n, s, BF16, kind="ExternalInput").ap()
    ident_h = drb("identc", [128, 128])
    ones32_h = drb("ones32c", [128, 32])
    allones_h = drb("allonesc", [128, 128])
    indcons_h = drb("indconsc", [128, 4 * 16])
    indmb_h = drb("indmbc", [64, 8 * 128])
    y_h = dr("y", [nb, T, E], "ExternalOutput")
    dbg_h = {}
    if dbg:
        for n, s in [("ktc", [128, NFT * T]), ("vc", [128, (T // 128) * E]),
                     ("qsb", [128, NFT * CH]), ("pvraw", [128, NFT * CH]),
                     ("scrA", [128, NET * CH]), ("shilo", [128, 4 * CH]),
                     ("mhilo", [64, CH]), ("tern", [128, NET * E]),
                     ("expb", [128, 4 * CH])]:
            dbg_h[n] = nc.dram_tensor("dbg_" + n, s, BF16, kind="ExternalOutput").ap()
        dbg_h["xr"] = nc.dram_tensor("dbg_xr", [128, 5 * E], F32, kind="ExternalOutput").ap()
        for n, s in [("delta", [128, 2]), ("aggr", [128, 8]), ("rstd", [128, 4]),
                     ("fhn", [16, 5 * CH]), ("fhnc", [16, 4]), ("tmp", [128, E]),
                     ("dsc", [128, 10])]:
            dbg_h[n] = nc.dram_tensor("dbg_" + n, s, F32, kind="ExternalOutput").ap()

    with ExitStack() as stk:
        e_ = stk.enter_context
        sb = lambda n, s, d: e_(nc.sbuf_tensor(n, s, d))
        wqkv = sb("wqkv", [128, NET * 3 * E], BF16)
        wout = sb("wout", [128, NET * E], BF16)
        tern = sb("tern", [128, NET * E], BF16)
        biasr = sb("biasr", [128, E], F32)
        alphar = sb("alphar", [128, 1], F32)
        ktc = sb("ktc", [128, NFT * T], BF16)
        vc = sb("vc", [128, (T // 128) * E], BF16)
        xr = sb("xr", [128, 5 * E], F32)
        xnst = sb("xnst", [128, 2 * E], BF16)
        scrA = sb("scrA", [128, NET * CH], BF16)
        qsb = sb("qsb", [128, NFT * CH], BF16)
        expb = sb("expb", [128, 4 * CH], BF16)
        pvraw = sb("pvraw", [128, NFT * CH], BF16)
        shilo = sb("shilo", [128, 4 * CH], BF16)
        mhilo = sb("mhilo", [64, CH], BF16)
        tmp = sb("tmp", [128, E], F32)
        stats = sb("stats", [128, 4 * 12], F32)
        aggr = sb("aggr", [128, 8], F32)
        nt1 = sb("nt1", [128, 4], F32)
        nt2 = sb("nt2", [128, 4], F32)
        rstd = sb("rstd", [128, 4], F32)
        fhn = sb("fhn", [16, 5 * CH], F32)
        fhnc = sb("fhnc", [16, 4], F32)
        abdt = sb("abdt_s", [16, 3], F32)
        delta = sb("delta", [128, 2], F32)
        dsc = sb("dsc", [128, 10], F32)
        idb = sb("idb", [128, 128], BF16)
        on32 = sb("on32", [128, 32], BF16)
        allone = sb("allone", [128, 128], BF16)
        indc = sb("indc", [128, 4 * 16], BF16)
        indm = sb("indm", [64, 8 * 128], BF16)

        pb = [e_(nc.psum_tensor(f"pb{i}", [128, 512], F32)) for i in range(8)]

        sem_names = ["te", "dve", "act", "pool", "sx", "sy", "sw"]
        sem = {n: e_(nc.semaphore(n)) for n in sem_names}
        cnt = {n: 0 for n in sem_names}
        waited = {}

        TEe, DV, AC, PO, SP = nc.tensor, nc.vector, nc.scalar, nc.gpsimd, nc.sync
        ENAME = {id(TEe): "TE", id(DV): "DV", id(AC): "AC", id(PO): "PO", id(SP): "SP"}

        def W(eng, s, val):
            if val <= 0:
                return
            key = (ENAME[id(eng)], s)
            if waited.get(key, 0) >= val:
                return
            assert val <= cnt[s], f"wait {key} {val} > emitted {cnt[s]}"
            eng.wait_ge(sem[s], val)
            waited[key] = val

        def I(instr, s, n=1):
            cnt[s] += n
            instr.then_inc(sem[s], n)
            return cnt[s]

        def dsw(instr):
            v = I(instr, "dve")
            W(DV, "dve", v)
            return v

        bdata = [0] * 8
        bfree = [[] for _ in range(8)]

        def bank_acquire(i):
            for (s, v) in bfree[i]:
                W(TEe, s, v)
            bfree[i] = []

        def bank_read(i, eng):
            W(eng, "te", bdata[i])

        xload = {}

        def emit_xload(b_, ci, tb):
            gtb = (b_ * NCH + ci) * NTB + tb
            slot = gtb % 5
            if gtb >= 5:
                W(SP, "sy", 16 * (gtb - 4))
            d = SP.dma_start(xr[:, slot * E:(slot + 1) * E],
                             x_h[b_, (ci * CH + tb * 128):(ci * CH + tb * 128 + 128), :])
            xload[gtb] = I(d, "sx", 16)

        for tb in range(NTB):
            emit_xload(0, 0, tb)

        I(SP.dma_start(idb[:], ident_h), "sw", 16)
        I(SP.dma_start(on32[:], ones32_h), "sw", 16)
        I(SP.dma_start(allone[:], allones_h), "sw", 16)
        I(SP.dma_start(indc[:], indcons_h), "sw", 16)
        I(SP.dma_start(indm[:], indmb_h), "sw", 16)
        I(SP.dma_start(abdt[:], abdt_h), "sw", 16)
        I(SP.dma_start(alphar[:], alpharep_h), "sw", 16)
        I(SP.dma_start(biasr[:], biasrep_h), "sw", 16)
        sw_consts = cnt["sw"]

        stag = pvraw[:].bitcast(F32)
        cast_dve = {}
        k = 0
        for et in range(NET):
            for sub in range(3):
                sl = k % 2
                if k >= 2:
                    W(SP, "dve", cast_dve[k - 2])
                d = I(SP.dma_start(stag[:, sl * E:(sl + 1) * E],
                                   wqkvT_h[et * 128:(et + 1) * 128, sub * E:(sub + 1) * E]),
                      "sw", 16)
                W(DV, "sw", d)
                r = DV.tensor_copy(wqkv[:, et * 3 * E + sub * E:et * 3 * E + (sub + 1) * E],
                                   stag[:, sl * E:(sl + 1) * E])
                cast_dve[k] = I(r, "dve")
                k += 1
        sw_wqkv = cnt["sw"]
        v_wqkv = cast_dve[k - 1]
        for et in range(NET):
            sl = k % 2
            W(SP, "dve", cast_dve[k - 2])
            d = I(SP.dma_start(stag[:, sl * E:(sl + 1) * E],
                               woutT_h[et * 128:(et + 1) * 128, :]), "sw", 16)
            W(DV, "sw", d)
            r = DV.tensor_copy(wout[:, et * E:(et + 1) * E], stag[:, sl * E:(sl + 1) * E])
            cast_dve[k] = I(r, "dve")
            k += 1
        sw_wout = cnt["sw"]
        v_wout = cast_dve[k - 1]

        red_dve = [0] * NET
        for et in range(NET):
            sl = k % 2
            W(SP, "dve", cast_dve[k - 2])
            d = I(SP.dma_start(stag[:, sl * E:(sl + 1) * E],
                               wffnT_h[et * 128:(et + 1) * 128, :]), "sw", 16)
            W(DV, "sw", d)
            r = DV.tensor_reduce(dsc[:, et:et + 1], stag[:, sl * E:(sl + 1) * E],
                                 AX.X, ALU.add, apply_absolute_value=True)
            red_dve[et] = dsw(r)
            cast_dve[k] = red_dve[et]
            k += 1
        dsw(DV.tensor_reduce(dsc[:, 8:9], dsc[:, 0:8], AX.X, ALU.add))
        phl = shilo
        dsw(DV.tensor_copy(phl[:, 0:1], dsc[:, 8:9]))
        dsw(DV.tensor_tensor(phl[:, 1:2], dsc[:, 8:9], phl[:, 0:1], ALU.subtract))
        v_phl = cnt["dve"]
        W(TEe, "sw", sw_consts)
        W(TEe, "dve", v_phl)
        bank_acquire(0)
        mm = TEe.matmul(pb[0][:, 0:2], allone[:], phl[:, 0:2], start=True, stop=True)
        bdata[0] = I(mm, "te")
        te_allones = bdata[0]
        bank_read(0, DV)
        dsw(DV.tensor_copy(dsc[:, 0:2], pb[0][:, 0:2]))
        bfree[0].append(("dve", cnt["dve"]))
        dsw(DV.tensor_tensor(dsc[:, 2:3], dsc[:, 0:1], dsc[:, 1:2], ALU.add))
        dsw(DV.tensor_scalar(delta[:, 0:1], dsc[:, 2:3], 0.7 / (1024.0 * 1024.0), None, ALU.mult))
        dsw(DV.tensor_scalar(delta[:, 1:2], delta[:, 0:1], -1.0, None, ALU.mult))
        v_delta = cnt["dve"]
        cmpv = expb[:].bitcast(F32)
        q2_dve = [0] * NET
        for et in range(NET):
            sl = k % 2
            W(SP, "dve", cast_dve[k - 2])
            d = I(SP.dma_start(stag[:, sl * E:(sl + 1) * E],
                               wffnT_h[et * 128:(et + 1) * 128, :]), "sw", 16)
            W(DV, "sw", d)
            W(DV, "dve", v_delta)
            s_ap = stag[:, sl * E:(sl + 1) * E]
            DV.tensor_scalar(cmpv[:, 0:E], s_ap, delta[:, 0:1], None, ALU.is_gt)
            DV.tensor_scalar(tmp[:, 0:E], s_ap, delta[:, 1:2], None, ALU.is_lt)
            q2_dve[et] = dsw(DV.tensor_tensor(tern[:, et * E:(et + 1) * E],
                                              cmpv[:, 0:E], tmp[:, 0:E], ALU.subtract))
            cast_dve[k] = q2_dve[et]
            k += 1
        v_tern = cnt["dve"]
        sw_wffn = cnt["sw"]

        W(DV, "sw", sw_consts)
        dsw(DV.tensor_scalar(fhnc[:, 0:1], abdt[:, 2:3], 1.0 / 12.5, None, ALU.mult))
        dsw(DV.tensor_tensor(fhnc[:, 1:2], fhnc[:, 0:1], abdt[:, 1:2], ALU.mult))
        dsw(DV.tensor_scalar(fhnc[:, 1:2], fhnc[:, 1:2], 1.0, None, ALU.add))
        dsw(DV.reciprocal(fhnc[:, 2:3], fhnc[:, 1:2]))
        dsw(DV.tensor_tensor(fhnc[:, 1:2], fhnc[:, 0:1], fhnc[:, 2:3], ALU.mult))
        dsw(DV.memset(fhnc[:, 3:4], -2.5))
        v_mh0 = dsw(DV.memset(mhilo[:], 0.0))
        v_fhnc = cnt["dve"]

        state = {
            "xnst_free": [("dve", 0), ("dve", 0)],
            "exp_guard": [("dve", v_tern)] * 4,
            "shilo_guard": [("te", te_allones), ("te", te_allones)],
            "mhilo_guard": ("dve", v_mh0),
        }
        y2_ms = {}

        def ln_newton(n, var_ap, out_ap):
            dsw(DV.tensor_scalar(nt1[:, 0:n], var_ap, 1e-5, None, ALU.add))
            dsw(DV.tensor_scalar(out_ap.bitcast(I32), nt1[:, 0:n].bitcast(I32),
                                 -0.5, MAGIC, ALU.mult, ALU.add))
            for _ in range(3):
                dsw(DV.tensor_tensor(nt2[:, 0:n], out_ap, out_ap, ALU.mult))
                dsw(DV.tensor_tensor(nt2[:, 0:n], nt2[:, 0:n], nt1[:, 0:n], ALU.mult))
                dsw(DV.tensor_scalar(nt2[:, 0:n], nt2[:, 0:n], -0.5, 1.5, ALU.mult, ALU.add))
                dsw(DV.tensor_tensor(out_ap, out_ap, nt2[:, 0:n], ALU.mult))

        def ln_stats(srcs):
            n = len(srcs)
            last = None
            for k, (h0, h1) in enumerate(srcs):
                DV.bn_stats(stats[:, k * 12:k * 12 + 6], h0)
                last = DV.bn_stats(stats[:, k * 12 + 6:k * 12 + 12], h1)
            dsw(last)
            for k in range(n):
                last = DV.bn_aggr(aggr[:, 2 * k:2 * k + 2], stats[:, k * 12:(k + 1) * 12])
            dsw(last)
            ln_newton(n, aggr[:, 1:2 * n:2], rstd[:, 0:n])

        def xn_transpose_phase(emit_ts):
            te_tb = [0] * NTB
            for bi in range(4):
                bank_acquire(bi)
            for tb in range(NTB):
                if tb >= 2:
                    guard = ("te", te_tb[tb - 2])
                else:
                    guard = state["xnst_free"][tb % 2]
                xm = emit_ts(tb, guard)
                W(TEe, "sw", sw_consts)
                W(TEe, "dve", xm)
                mm = None
                for et in range(NET):
                    bi = et % 4
                    pbf = pb[bi][:].bitcast(BF16)
                    mm = TEe.transpose(
                        pbf[:, (et // 4) * 512 + tb * 128:(et // 4) * 512 + (tb + 1) * 128],
                        xnst[:, (tb % 2) * E + et * 128:(tb % 2) * E + et * 128 + 128],
                        idb[:])
                te_tb[tb] = I(mm, "te")
            last = 0
            for et in range(NET):
                bi = et % 4
                bdata[bi] = te_tb[NTB - 1]
                bank_read(bi, DV)
                pbf = pb[bi][:].bitcast(BF16)
                r = DV.tensor_copy(scrA[:, et * CH:(et + 1) * CH],
                                   pbf[:, (et // 4) * 512:(et // 4) * 512 + 512])
                last = I(r, "dve")
                bfree[bi].append(("dve", last))
            state["xnst_free"] = [("te", te_tb[2]), ("te", te_tb[3])]
            return last

        def chunk(b_, ci):
            gt = b_ * NCH + ci
            njs = 4 * (ci + 1)
            if gt + 1 < nb * NCH:
                b2, c2 = divmod(gt + 1, NCH)
                emit_xload(b2, c2, 0)

            slot_of = lambda tb: ((gt * NTB + tb) % 5) * E
            for tb in range(NTB):
                W(DV, "sx", xload[gt * NTB + tb])
            ln_stats([(xr[:, slot_of(tb):slot_of(tb) + 512],
                       xr[:, slot_of(tb) + 512:slot_of(tb) + E]) for tb in range(NTB)])
            def emit_xn(tb, guard):
                W(DV, guard[0], guard[1])
                sl2 = (tb % 2) * E
                r = DV.tensor_scalar(xnst[:, sl2:sl2 + E], xr[:, slot_of(tb):slot_of(tb) + E],
                                     aggr[:, 2 * tb:2 * tb + 1], rstd[:, tb:tb + 1],
                                     ALU.subtract, ALU.mult)
                return I(r, "dve")
            scr_w = xn_transpose_phase(emit_xn)

            W(TEe, "sw", sw_wqkv)
            W(TEe, "dve", scr_w)
            qk_ev = []
            for fb in range(16):
                bi = fb % 4
                bank_acquire(bi)
                mm = None
                for et in range(NET):
                    mm = TEe.matmul(pb[bi][:],
                                    wqkv[:, et * 3 * E + fb * 128:et * 3 * E + fb * 128 + 128],
                                    scrA[:, et * CH:(et + 1) * CH],
                                    start=(et == 0), stop=(et == NET - 1))
                bdata[bi] = I(mm, "te")
                bank_read(bi, AC)
                if fb < 8:
                    dst = qsb[:, fb * CH:(fb + 1) * CH]
                else:
                    ft = fb - 8
                    dst = ktc[:, ft * T + ci * CH: ft * T + (ci + 1) * CH]
                qk_ev.append(I(AC.copy(dst, pb[bi][:]), "act"))
                bfree[bi].append(("act", qk_ev[-1]))
            v_ev = []
            for tb in range(NTB):
                for half in range(2):
                    bi = 4 + 2 * (tb % 2) + half
                    bank_acquire(bi)
                    mm = None
                    for et in range(NET):
                        mm = TEe.matmul(
                            pb[bi][:],
                            scrA[:, et * CH + tb * 128:et * CH + tb * 128 + 128],
                            wqkv[:, et * 3 * E + 2 * E + half * 512:et * 3 * E + 2 * E + (half + 1) * 512],
                            start=(et == 0), stop=(et == NET - 1))
                    bdata[bi] = I(mm, "te")
                    bank_read(bi, DV)
                    j = 4 * ci + tb
                    r = DV.tensor_copy(vc[:, j * E + half * 512:j * E + (half + 1) * 512], pb[bi][:])
                    v_ev.append(I(r, "dve"))
                    bfree[bi].append(("dve", v_ev[-1]))
            scr_qkv_readers = cnt["te"]

            W(TEe, "act", max(qk_ev))
            W(TEe, "dve", max(v_ev))
            sc_ms = {}
            exp_ms = {}

            def emit_scores(ft, j):
                mm = None
                for hh in range(2):
                    bi = 2 * (j % 2) + hh
                    bank_acquire(bi)
                    mm = TEe.matmul(pb[bi][:],
                                    ktc[hh * 64:hh * 64 + 64, ft * T + j * 128:ft * T + (j + 1) * 128],
                                    qsb[hh * 64:hh * 64 + 64, ft * CH:(ft + 1) * CH],
                                    start=True, stop=True)
                v = I(mm, "te")
                sc_ms[(ft, j)] = v
                bdata[2 * (j % 2)] = v
                bdata[2 * (j % 2) + 1] = v

            def emit_exp(ft, j):
                sl = j % 2
                W(AC, "te", sc_ms[(ft, j)])
                for k in (2 * sl, 2 * sl + 1):
                    s_, v_ = state["exp_guard"][k]
                    W(AC, s_, v_)
                AC.activation(expb[:, (2 * sl) * CH:(2 * sl + 1) * CH], pb[2 * (j % 2)][:],
                              AF.Exp, scale=0.125)
                r = AC.activation(expb[:, (2 * sl + 1) * CH:(2 * sl + 2) * CH],
                                  pb[2 * (j % 2) + 1][:], AF.Exp, scale=0.125)
                exp_ms[(ft, j)] = I(r, "act", 2)
                bfree[2 * (j % 2)].append(("act", exp_ms[(ft, j)]))
                bfree[2 * (j % 2) + 1].append(("act", exp_ms[(ft, j)]))

            def emit_pvs(p, ft, j):
                sl = j % 2
                W(TEe, "act", exp_ms[(ft, j)])
                bv = 4 + (p % 2)
                if j == 0:
                    bank_acquire(bv)
                    if p == 0:
                        bank_acquire(6)
                eA = expb[:, (2 * sl) * CH:(2 * sl + 1) * CH]
                eB = expb[:, (2 * sl + 1) * CH:(2 * sl + 2) * CH]
                TEe.matmul(pb[bv][0:64, :], vc[:, j * E + (2 * ft) * 64:j * E + (2 * ft) * 64 + 64],
                           eA, start=(j == 0), stop=(j == njs - 1))
                TEe.matmul(pb[bv][64:128, :],
                           vc[:, j * E + (2 * ft + 1) * 64:j * E + (2 * ft + 1) * 64 + 64],
                           eB, start=(j == 0), stop=(j == njs - 1))
                hg = 2 * p
                TEe.matmul(pb[6][32 * hg:32 * hg + 32, :], on32[:, 0:32], eA,
                           start=(j == 0), stop=(j == njs - 1), tile_position=(0, 32 * hg))
                mm = TEe.matmul(pb[6][32 * hg + 32:32 * hg + 64, :], on32[:, 0:32], eB,
                                start=(j == 0), stop=(j == njs - 1),
                                tile_position=(0, 32 * hg + 32))
                v = I(mm, "te")
                state["exp_guard"][2 * sl] = ("te", v)
                state["exp_guard"][2 * sl + 1] = ("te", v)
                if j == njs - 1:
                    bdata[bv] = v
                    if p == 1:
                        bdata[6] = v

            pv_ev = [0] * NFT
            cons_te = 0
            for g in range(4):
                for p in range(2):
                    ft = 2 * g + p
                    emit_scores(ft, 0)
                    for j in range(njs):
                        if j + 1 < njs:
                            emit_scores(ft, j + 1)
                        emit_exp(ft, j)
                        emit_pvs(p, ft, j)
                    bv = 4 + (p % 2)
                    bank_read(bv, DV)
                    r = DV.tensor_copy(pvraw[:, ft * CH:(ft + 1) * CH], pb[bv][:])
                    pv_ev[ft] = I(r, "dve")
                    bfree[bv].append(("dve", pv_ev[ft]))
                gp = g % 2
                bank_read(6, DV)
                s_, v_ = state["shilo_guard"][gp]
                W(DV, s_, v_)
                DV.tensor_copy(shilo[:, (2 * gp) * CH:(2 * gp + 1) * CH], pb[6][:])
                r = DV.tensor_tensor(shilo[:, (2 * gp + 1) * CH:(2 * gp + 2) * CH], pb[6][:],
                                     shilo[:, (2 * gp) * CH:(2 * gp + 1) * CH], ALU.subtract)
                sev = I(r, "dve")
                bfree[6].append(("dve", sev))
                if g == 0:
                    bank_acquire(7)
                W(TEe, "dve", sev)
                TEe.matmul(pb[7][0:16, :], indc[:, g * 16:(g + 1) * 16],
                           shilo[:, (2 * gp) * CH:(2 * gp + 1) * CH],
                           start=(g == 0), stop=False)
                mm = TEe.matmul(pb[7][0:16, :], indc[:, g * 16:(g + 1) * 16],
                                shilo[:, (2 * gp + 1) * CH:(2 * gp + 2) * CH],
                                start=False, stop=(g == 3))
                cons_te = I(mm, "te")
                state["shilo_guard"][gp] = ("te", cons_te)
            bdata[7] = cons_te

            F = lambda k: fhn[:, k * CH:(k + 1) * CH]
            bank_read(7, DV)
            W(DV, "dve", v_fhnc)
            DV.reciprocal(F(0), pb[7][0:16, :])
            DV.tensor_tensor(F(1), pb[7][0:16, :], F(0), ALU.mult)
            DV.tensor_scalar(F(2), F(1), 1e-6, None, ALU.max)
            DV.tensor_scalar(F(3), F(2), -1.0, 2.0, ALU.mult, ALU.add)
            r = DV.tensor_tensor(F(3), F(1), F(3), ALU.mult)
            v_sn = I(r, "dve")
            W(AC, "dve", v_sn)
            v_t = I(AC.activation(F(4), F(2), AF.Tanh, scale=5.0, bias=fhnc[:, 3:4]), "act")
            W(DV, "act", v_t)
            DV.tensor_scalar(F(4), F(4), 0.45, 0.55, ALU.mult, ALU.add)
            DV.tensor_tensor(F(4), F(3), F(4), ALU.mult)
            DV.tensor_scalar(F(3), F(4), abdt[:, 2:3], None, ALU.mult)
            r = DV.tensor_scalar(F(3), F(3), 3.0, -3.0, ALU.min, ALU.max)
            v_v1 = I(r, "dve")
            DV.tensor_scalar(F(0), F(3), abdt[:, 0:1], fhnc[:, 1:2], ALU.add, ALU.mult)
            DV.tensor_scalar(F(0), F(0), 3.0, -3.0, ALU.min, ALU.max)
            W(AC, "dve", v_v1)
            v_sq = I(AC.activation(F(2), F(3), AF.Square), "act")
            W(DV, "act", v_sq)
            DV.scalar_tensor_tensor(F(2), F(2), 1.0 / 3.0, F(3), ALU.mult, ALU.mult)
            DV.tensor_tensor(F(2), F(3), F(2), ALU.subtract)
            DV.tensor_tensor(F(2), F(2), F(0), ALU.subtract)
            DV.tensor_tensor(F(2), F(2), F(4), ALU.add)
            DV.scalar_tensor_tensor(F(2), F(2), abdt[:, 2:3], F(3), ALU.mult, ALU.add)
            DV.tensor_scalar(F(2), F(2), 3.0, -3.0, ALU.min, ALU.max)
            DV.tensor_scalar(F(0), F(1), 1e-6, None, ALU.max)
            r = DV.tensor_tensor(F(2), F(2), F(0), ALU.mult)
            v_resp = I(r, "dve")
            W(AC, "dve", v_resp)
            v_t2 = I(AC.activation(F(2), F(2), AF.Tanh, scale=0.5), "act")
            W(DV, "act", v_t2)
            DV.tensor_scalar(F(2), F(2), 0.25, 0.75, ALU.mult, ALU.add)
            DV.tensor_tensor(F(3), F(2), F(1), ALU.mult)
            DV.tensor_scalar(F(3), F(3), 1e-8, None, ALU.add)
            r = DV.tensor_tensor(F(3), F(3), pb[7][0:16, :], ALU.mult)
            vD = I(r, "dve")
            bfree[7].append(("dve", vD))
            DV.reciprocal(F(3), F(3))
            DV.tensor_tensor(F(3), F(2), F(3), ALU.mult)
            s_, v_ = state["mhilo_guard"]
            W(DV, s_, v_)
            DV.tensor_copy(mhilo[0:16, :], F(3))
            r = DV.tensor_tensor(mhilo[32:48, :], F(3), mhilo[0:16, :], ALU.subtract)
            m_ms = I(r, "dve")

            at_ms = [0] * NFT
            for ft in range(NFT):
                bi = ft % 2
                bank_acquire(bi)
                W(TEe, "dve", m_ms)
                mm = TEe.matmul(pb[bi][:], indm[0:64, ft * 128:(ft + 1) * 128],
                                mhilo[0:64, :], start=True, stop=True)
                bdata[bi] = I(mm, "te")
                bank_read(bi, DV)
                W(DV, "te", scr_qkv_readers)
                r = DV.tensor_tensor(scrA[:, ft * CH:(ft + 1) * CH],
                                     pvraw[:, ft * CH:(ft + 1) * CH], pb[bi][:], ALU.mult)
                at_ms[ft] = I(r, "dve")
                bfree[bi].append(("dve", at_ms[ft]))
            state["mhilo_guard"] = ("te", cnt["te"])

            W(TEe, "sw", sw_wout)
            W(TEe, "dve", max(at_ms))
            for tb in range(NTB):
                for half in range(2):
                    bi = 2 * tb + half
                    bank_acquire(bi)
                    mm = None
                    for ft in range(NFT):
                        mm = TEe.matmul(pb[bi][:],
                                        scrA[:, ft * CH + tb * 128:ft * CH + tb * 128 + 128],
                                        wout[:, ft * E + half * 512:ft * E + (half + 1) * 512],
                                        start=(ft == 0), stop=(ft == NFT - 1))
                    bdata[bi] = I(mm, "te")

            for tb in range(NTB):
                bank_read(2 * tb, DV)
                bank_read(2 * tb + 1, DV)
            ln_stats([(pb[2 * tb][:], pb[2 * tb + 1][:]) for tb in range(NTB)])
            for tb in range(NTB):
                for half in range(2):
                    DV.tensor_scalar(tmp[:, half * 512:(half + 1) * 512], pb[2 * tb + half][:],
                                     aggr[:, 2 * tb:2 * tb + 1], rstd[:, tb:tb + 1],
                                     ALU.subtract, ALU.mult)
                r = DV.tensor_tensor(xr[:, slot_of(tb):slot_of(tb) + E],
                                     xr[:, slot_of(tb):slot_of(tb) + E], tmp[:, 0:E], ALU.add)
                v = I(r, "dve")
                bfree[2 * tb].append(("dve", v))
                bfree[2 * tb + 1].append(("dve", v))

            ln_stats([(xr[:, slot_of(tb):slot_of(tb) + 512],
                       xr[:, slot_of(tb) + 512:slot_of(tb) + E]) for tb in range(NTB)])
            scr_w2 = xn_transpose_phase(emit_xn)

            def out3_slot(tb):
                if tb == 0:
                    return tmp[:, 0:E]
                if tb <= 2:
                    return stag[:, (tb - 1) * E:tb * E]
                return cmpv[:, 0:E]

            W(TEe, "sw", sw_wffn)
            W(TEe, "dve", max(v_tern, scr_w2))
            o3_ms = [0] * NTB
            for tb in range(NTB):
                evs = []
                for half in range(2):
                    bi = 4 + 2 * (tb % 2) + half
                    bank_acquire(bi)
                    mm = None
                    for et in range(NET):
                        mm = TEe.matmul(
                            pb[bi][:],
                            scrA[:, et * CH + tb * 128:et * CH + tb * 128 + 128],
                            tern[:, et * E + half * 512:et * E + (half + 1) * 512],
                            start=(et == 0), stop=(et == NET - 1))
                    bdata[bi] = I(mm, "te")
                o3 = out3_slot(tb)
                W(DV, "sw", sw_consts)
                if tb == 3:
                    for k in range(4):
                        s_, v_ = state["exp_guard"][k]
                        W(DV, s_, v_)
                for half in range(2):
                    bi = 4 + 2 * (tb % 2) + half
                    bank_read(bi, DV)
                    r = DV.scalar_tensor_tensor(o3[:, half * 512:(half + 1) * 512],
                                                pb[bi][:], alphar[:, 0:1],
                                                biasr[:, half * 512:(half + 1) * 512],
                                                ALU.mult, ALU.add)
                    evs.append(I(r, "dve"))
                    bfree[bi].append(("dve", evs[-1]))
                o3_ms[tb] = evs[-1]

            ln_stats([(out3_slot(tb)[:, 0:512], out3_slot(tb)[:, 512:E]) for tb in range(NTB)])
            for tb in range(NTB):
                gtb = gt * NTB + tb
                o3 = out3_slot(tb)
                DV.tensor_scalar(o3[:, 0:E], o3[:, 0:E],
                                 aggr[:, 2 * tb:2 * tb + 1], rstd[:, tb:tb + 1],
                                 ALU.subtract, ALU.mult)
                r = DV.tensor_tensor(xr[:, slot_of(tb):slot_of(tb) + E],
                                     xr[:, slot_of(tb):slot_of(tb) + E], o3[:, 0:E], ALU.add)
                y2_ms[gtb] = I(r, "dve")
                if tb == 3:
                    for k in range(4):
                        state["exp_guard"][k] = ("dve", y2_ms[gtb])
                W(SP, "dve", y2_ms[gtb])
                d = SP.dma_start(y_h[b_, (ci * CH + tb * 128):(ci * CH + tb * 128 + 128), :],
                                 xr[:, slot_of(tb):slot_of(tb) + E])
                I(d, "sy", 16)
                if gt + 1 < nb * NCH and tb < NTB - 1:
                    b2, c2 = divmod(gt + 1, NCH)
                    emit_xload(b2, c2, tb + 1)

        for b_ in range(nb):
            for ci in range(NCH):
                chunk(b_, ci)

        if dbg:
            W(SP, "dve", cnt["dve"])
            W(SP, "act", cnt["act"])
            W(SP, "te", cnt["te"])
            named = {"ktc": ktc, "vc": vc, "qsb": qsb, "pvraw": pvraw, "scrA": scrA,
                     "shilo": shilo, "mhilo": mhilo, "tern": tern, "xr": xr,
                     "expb": expb, "delta": delta, "aggr": aggr, "rstd": rstd,
                     "fhn": fhn, "fhnc": fhnc, "tmp": tmp, "dsc": dsc}
            for n, hh in dbg_h.items():
                I(SP.dma_start(hh, named[n][:]), "sy", 16)
        W(SP, "sy", cnt["sy"])

    return nc


def _make_consts():
    ident = np.eye(128, dtype=np.float32)
    ones32 = np.ones((128, 32), np.float32)
    allones = np.ones((128, 128), np.float32)
    indcons = np.zeros((128, 4 * 16), np.float32)
    for g in range(4):
        for m in range(16):
            if 4 * g <= m < 4 * g + 4:
                indcons[32 * (m - 4 * g), g * 16 + m] = 1.0
    indmb = np.zeros((64, 8 * 128), np.float32)
    for ft in range(8):
        for p in range(128):
            h = 2 * ft + (1 if p >= 64 else 0)
            indmb[h, ft * 128 + p] = 1.0
            indmb[32 + h, ft * 128 + p] = 1.0
    return ident, ones32, allones, indcons, indmb


def kernel(L0_attn_x, L0_attn_spectral_basis, L0_attn_W_qkv, L0_attn_W_out,
           L0_attn_a, L0_attn_b, L0_attn_dt, L0_ffn_weight, L0_ffn_alpha,
           L0_ffn_bias):
    x = np.ascontiguousarray(np.asarray(L0_attn_x, dtype=np.float32))
    wqkvT = np.ascontiguousarray(np.asarray(L0_attn_W_qkv, np.float32).T)
    woutT = np.ascontiguousarray(np.asarray(L0_attn_W_out, np.float32).T)
    wffnT = np.ascontiguousarray(np.asarray(L0_ffn_weight, np.float32).T)
    bias = np.asarray(L0_ffn_bias, np.float32).reshape(1, E)
    biasrep = np.ascontiguousarray(np.broadcast_to(bias, (128, E)))
    alpharep = np.ascontiguousarray(
        np.broadcast_to(np.asarray(L0_ffn_alpha, np.float32).reshape(1, 1), (128, 1)))
    abdt = np.ascontiguousarray(np.broadcast_to(
        np.stack([np.asarray(L0_attn_a, np.float32).reshape(()),
                  np.asarray(L0_attn_b, np.float32).reshape(()),
                  np.asarray(L0_attn_dt, np.float32).reshape(())]).reshape(1, 3), (16, 3)))
    ident, ones32, allones, indcons, indmb = [
        a.astype(ml_dtypes.bfloat16) for a in _make_consts()]

    nc = build_program(BPC)
    in_maps = []
    for c in range(NCORES):
        in_maps.append({
            "x": np.ascontiguousarray(x[c * BPC:(c + 1) * BPC]),
            "wqkvT": wqkvT, "woutT": woutT, "wffnT": wffnT,
            "biasrep": biasrep, "alpharep": alpharep, "abdt": abdt,
            "identc": ident, "ones32c": ones32, "allonesc": allones,
            "indconsc": indcons, "indmbc": indmb,
        })
    res = run_bass_kernel_spmd(nc, in_maps, core_ids=list(range(NCORES)))
    y = np.concatenate([np.asarray(r["y"]) for r in res.results], axis=0)
    return y.astype(np.float32)


# version 05 (final)
# speedup vs baseline: 1.1366x; 1.1366x over previous
import numpy as np
import ml_dtypes
from contextlib import ExitStack

import concourse.bass as bass
import concourse.mybir as mybir
from concourse.bass_utils import run_bass_kernel_spmd

F32 = mybir.dt.float32
BF16 = mybir.dt.bfloat16
I32 = mybir.dt.int32
AF = mybir.ActivationFunctionType
ALU = mybir.AluOpType
AX = mybir.AxisListType

B, T, E = 32, 2048, 1024
H, DH, CH = 16, 64, 512
NCH = T // CH
NTB = CH // 128
NET = E // 128
NFT = H // 2
NCORES = 8
BPC = B // NCORES

MAGIC = 1597463007.0


def build_program(nb=BPC, dbg=False):
    nc = bass.Bass("TRN2", target_bir_lowering=False, dynamic_dma_scratch_size=512)

    dr = lambda n, s, k: nc.dram_tensor(n, s, F32, kind=k).ap()
    x_h = dr("x", [nb, T, E], "ExternalInput")
    wqkvT_h = dr("wqkvT", [E, 3 * E], "ExternalInput")
    woutT_h = dr("woutT", [E, E], "ExternalInput")
    wffnT_h = dr("wffnT", [E, E], "ExternalInput")
    biasrep_h = dr("biasrep", [128, E], "ExternalInput")
    alpharep_h = dr("alpharep", [128, 1], "ExternalInput")
    abdt_h = dr("abdt", [16, 3], "ExternalInput")
    drb = lambda n, s: nc.dram_tensor(n, s, BF16, kind="ExternalInput").ap()
    ident_h = drb("identc", [128, 128])
    ones32_h = drb("ones32c", [128, 32])
    allones_h = drb("allonesc", [128, 128])
    indcons_h = drb("indconsc", [128, 4 * 16])
    indmb_h = drb("indmbc", [64, 8 * 128])
    y_h = dr("y", [nb, T, E], "ExternalOutput")
    vhbm = nc.dram_tensor("vhbm", [E, E], BF16, kind="Internal").ap()
    wohbm = nc.dram_tensor("wohbm", [E, E], BF16, kind="Internal").ap()

    dbg_h = {}
    if dbg:
        for n, s in [("ktc", [128, NFT * T]), ("vc", [128, (T // 128) * E]),
                     ("qsb", [128, NFT * CH]), ("pvraw", [128, NFT * CH]),
                     ("scrA", [128, NET * CH]), ("scrB", [128, NET * CH]),
                     ("shilo", [128, 4 * CH]), ("mhilo", [64, CH]),
                     ("tern", [128, NET * E]), ("expb", [128, 4 * CH]),
                     ("wqkv", [128, NET * 2 * E]), ("vdb", [128, 2 * E]),
                     ("wdb", [128, 2 * E])]:
            dbg_h[n] = nc.dram_tensor("dbg_" + n, s, BF16, kind="ExternalOutput").ap()
        for n, s in [("delta", [128, 2]), ("aggr", [128, 8]), ("rstd", [128, 4]),
                     ("fhn", [16, 5 * CH]), ("fhnc", [16, 4]), ("tmp", [128, E]),
                     ("dsc", [128, 10]), ("xr", [128, 8 * E])]:
            dbg_h[n] = nc.dram_tensor("dbg_" + n, s, F32, kind="ExternalOutput").ap()

    with ExitStack() as stk:
        e_ = stk.enter_context
        sb = lambda n, s, d: e_(nc.sbuf_tensor(n, s, d))
        wqkv = sb("wqkv", [128, NET * 2 * E], BF16)
        tern = sb("tern", [128, NET * E], BF16)
        vdb = sb("vdb", [128, 2 * E], BF16)
        wdb = sb("wdb", [128, 2 * E], BF16)
        biasr = sb("biasr", [128, E], F32)
        alphar = sb("alphar", [128, 1], F32)
        ktc = sb("ktc", [128, NFT * T], BF16)
        vc = sb("vc", [128, (T // 128) * E], BF16)
        xr = sb("xr", [128, 8 * E], F32)
        xnst = sb("xnst", [128, 4 * E], BF16)
        scrA = sb("scrA", [128, NET * CH], BF16)
        scrB = sb("scrB", [128, NET * CH], BF16)
        qsb = sb("qsb", [128, NFT * CH], BF16)
        expb = sb("expb", [128, 4 * CH], BF16)
        pvraw = sb("pvraw", [128, NFT * CH], BF16)
        shilo = sb("shilo", [128, 4 * CH], BF16)
        mhilo = sb("mhilo", [64, CH], BF16)
        tmp = sb("tmp", [128, E], F32)
        stats = sb("stats", [128, 4 * 12], F32)
        aggr = sb("aggr", [128, 8], F32)
        nt1 = sb("nt1", [128, 4], F32)
        nt2 = sb("nt2", [128, 4], F32)
        rstd = sb("rstd", [128, 4], F32)
        fhn = sb("fhn", [16, 5 * CH], F32)
        fhnc = sb("fhnc", [16, 4], F32)
        abdt = sb("abdt_s", [16, 3], F32)
        delta = sb("delta", [128, 2], F32)
        dsc = sb("dsc", [128, 10], F32)
        idb = sb("idb", [128, 128], BF16)
        on32 = sb("on32", [128, 32], BF16)
        allone = sb("allone", [128, 128], BF16)
        indc = sb("indc", [128, 4 * 16], BF16)
        indm = sb("indm", [64, 8 * 128], BF16)

        pb = [e_(nc.psum_tensor(f"pb{i}", [128, 512], F32)) for i in range(8)]

        sem_names = (["te", "dve", "act", "sc"]
                     + [f"xr{i}" for i in range(8)]
                     + ["sv0", "sv1", "sw0", "sw1", "sg0", "sg1", "so0", "so1"])
        sem = {n: e_(nc.semaphore(n)) for n in sem_names}
        cnt = {n: 0 for n in sem_names}
        waited = {}

        TEe, DV, AC, SP = nc.tensor, nc.vector, nc.scalar, nc.sync
        ENAME = {id(TEe): "TE", id(DV): "DV", id(AC): "AC", id(SP): "SP"}

        def W(eng, s, val):
            if val <= 0:
                return
            key = (ENAME[id(eng)], s)
            if waited.get(key, 0) >= val:
                return
            assert val <= cnt[s], f"wait {key} {val} > emitted {cnt[s]}"
            eng.wait_ge(sem[s], val)
            waited[key] = val

        def I(instr, s, n=1):
            cnt[s] += n
            instr.then_inc(sem[s], n)
            return cnt[s]

        def dsw(instr):
            v = I(instr, "dve")
            W(DV, "dve", v)
            return v

        bdata = [0] * 8
        bfree = [[] for _ in range(8)]

        def bank_acquire(i):
            for (s, v) in bfree[i]:
                W(TEe, s, v)
            bfree[i] = []

        def bank_read(i, eng):
            W(eng, "te", bdata[i])

        xload = {}

        def emit_xload(b_, ci, tb):
            gtb = (b_ * NCH + ci) * NTB + tb
            slot = gtb % 8
            sname = f"xr{slot}"
            if gtb >= 8:
                W(SP, sname, cnt[sname])
            d = SP.dma_start(xr[:, slot * E:(slot + 1) * E],
                             x_h[b_, (ci * CH + tb * 128):(ci * CH + tb * 128 + 128), :])
            xload[gtb] = (sname, I(d, sname, 16))

        for tb in range(NTB):
            emit_xload(0, 0, tb)

        I(SP.dma_start(idb[:], ident_h), "sc", 16)
        I(SP.dma_start(on32[:], ones32_h), "sc", 16)
        I(SP.dma_start(allone[:], allones_h), "sc", 16)
        I(SP.dma_start(indc[:], indcons_h), "sc", 16)
        I(SP.dma_start(indm[:], indmb_h), "sc", 16)
        I(SP.dma_start(abdt[:], abdt_h), "sc", 16)
        I(SP.dma_start(alphar[:], alpharep_h), "sc", 16)
        I(SP.dma_start(biasr[:], biasrep_h), "sc", 16)
        sw_consts = cnt["sc"]

        stag = pvraw[:].bitcast(F32)
        cast_dve = {}
        k = 0
        for et in range(NET):
            for sub in range(2):
                sl = k % 2
                if k >= 2:
                    W(SP, "dve", cast_dve[k - 2])
                d = I(SP.dma_start(stag[:, sl * E:(sl + 1) * E],
                                   wqkvT_h[et * 128:(et + 1) * 128, sub * E:(sub + 1) * E]),
                      f"sg{sl}", 16)
                W(DV, f"sg{sl}", d)
                r = DV.tensor_copy(wqkv[:, et * 2 * E + sub * E:et * 2 * E + (sub + 1) * E],
                                   stag[:, sl * E:(sl + 1) * E])
                cast_dve[k] = I(r, "dve")
                k += 1

        def hbm_convert(src_col0, dst_hbm, outs):
            nonlocal k
            for et in range(NET):
                sl = k % 2
                W(SP, "dve", cast_dve.get(k - 2, 0))
                if src_col0 is None:
                    src = woutT_h[et * 128:(et + 1) * 128, :]
                else:
                    src = wqkvT_h[et * 128:(et + 1) * 128, src_col0:src_col0 + E]
                d = I(SP.dma_start(stag[:, sl * E:(sl + 1) * E], src), f"sg{sl}", 16)
                W(DV, f"sg{sl}", d)
                if (k - 2) in outs:
                    W(DV, f"so{sl}", outs[k - 2])
                r = DV.tensor_copy(scrB[:, sl * E:(sl + 1) * E], stag[:, sl * E:(sl + 1) * E])
                cast_dve[k] = I(r, "dve")
                W(SP, "dve", cast_dve[k])
                outs[k] = I(SP.dma_start(dst_hbm[et * 128:(et + 1) * 128, :],
                                         scrB[:, sl * E:(sl + 1) * E]), f"so{sl}", 16)
                k += 1
            return outs[k - 1]

        _hbm_outs = {}
        hbm_convert(2 * E, vhbm, _hbm_outs)
        sw_vhbm = (cnt["so0"], cnt["so1"])
        hbm_convert(None, wohbm, _hbm_outs)
        sw_wohbm = (cnt["so0"], cnt["so1"])

        for et in range(NET):
            sl = k % 2
            W(SP, "dve", cast_dve[k - 2])
            d = I(SP.dma_start(stag[:, sl * E:(sl + 1) * E],
                               wffnT_h[et * 128:(et + 1) * 128, :]), f"sg{sl}", 16)
            W(DV, f"sg{sl}", d)
            r = DV.tensor_reduce(dsc[:, et:et + 1], stag[:, sl * E:(sl + 1) * E],
                                 AX.X, ALU.add, apply_absolute_value=True)
            cast_dve[k] = dsw(r)
            k += 1
        dsw(DV.tensor_reduce(dsc[:, 8:9], dsc[:, 0:8], AX.X, ALU.add))
        phl = shilo
        dsw(DV.tensor_copy(phl[:, 0:1], dsc[:, 8:9]))
        dsw(DV.tensor_tensor(phl[:, 1:2], dsc[:, 8:9], phl[:, 0:1], ALU.subtract))
        v_phl = cnt["dve"]
        W(TEe, "sc", sw_consts)
        W(TEe, "dve", v_phl)
        bank_acquire(0)
        mm = TEe.matmul(pb[0][:, 0:2], allone[:], phl[:, 0:2], start=True, stop=True)
        bdata[0] = I(mm, "te")
        te_allones = bdata[0]
        bank_read(0, DV)
        dsw(DV.tensor_copy(dsc[:, 0:2], pb[0][:, 0:2]))
        bfree[0].append(("dve", cnt["dve"]))
        dsw(DV.tensor_tensor(dsc[:, 2:3], dsc[:, 0:1], dsc[:, 1:2], ALU.add))
        dsw(DV.tensor_scalar(delta[:, 0:1], dsc[:, 2:3], 0.7 / (1024.0 * 1024.0), None, ALU.mult))
        dsw(DV.tensor_scalar(delta[:, 1:2], delta[:, 0:1], -1.0, None, ALU.mult))
        v_delta = cnt["dve"]
        cmpv = expb[:].bitcast(F32)
        for et in range(NET):
            sl = k % 2
            W(SP, "dve", cast_dve[k - 2])
            d = I(SP.dma_start(stag[:, sl * E:(sl + 1) * E],
                               wffnT_h[et * 128:(et + 1) * 128, :]), f"sg{sl}", 16)
            W(DV, f"sg{sl}", d)
            W(DV, "dve", v_delta)
            s_ap = stag[:, sl * E:(sl + 1) * E]
            DV.tensor_scalar(cmpv[:, 0:E], s_ap, delta[:, 0:1], None, ALU.is_gt)
            dsw(DV.tensor_scalar(tmp[:, 0:E], s_ap, delta[:, 1:2], None, ALU.is_lt))
            cast_dve[k] = dsw(DV.tensor_tensor(tern[:, et * E:(et + 1) * E],
                                               cmpv[:, 0:E], tmp[:, 0:E], ALU.subtract))
            k += 1
        v_tern = cnt["dve"]

        W(DV, "sc", sw_consts)
        dsw(DV.tensor_scalar(fhnc[:, 0:1], abdt[:, 2:3], 1.0 / 12.5, None, ALU.mult))
        dsw(DV.tensor_tensor(fhnc[:, 1:2], fhnc[:, 0:1], abdt[:, 1:2], ALU.mult))
        dsw(DV.tensor_scalar(fhnc[:, 1:2], fhnc[:, 1:2], 1.0, None, ALU.add))
        dsw(DV.reciprocal(fhnc[:, 2:3], fhnc[:, 1:2]))
        dsw(DV.tensor_tensor(fhnc[:, 1:2], fhnc[:, 0:1], fhnc[:, 2:3], ALU.mult))
        dsw(DV.memset(fhnc[:, 3:4], -2.5))
        v_mh0 = dsw(DV.memset(mhilo[:], 0.0))
        v_fhnc = cnt["dve"]

        state = {
            "xnst_free": [("dve", 0)] * 4,
            "exp_guard": [("dve", v_tern)] * 4,
            "shilo_guard": [("te", te_allones), ("te", te_allones)],
            "mhilo_guard": ("dve", v_mh0),
            "bufA_readers": [0, 0],
            "vdb_guard": [0, 0],
            "wdb_guard": [0, 0],
            "q_ms": 0,
            "v_ms": 0,
        }
        y2_ms = {}

        bufA = lambda c: scrA if (c % 2 == 0) else scrB

        def ln_newton(n, var_ap, out_ap):
            dsw(DV.tensor_scalar(nt1[:, 0:n], var_ap, 1e-5, None, ALU.add))
            dsw(DV.tensor_scalar(out_ap.bitcast(I32), nt1[:, 0:n].bitcast(I32),
                                 -0.5, MAGIC, ALU.mult, ALU.add))
            for _ in range(3):
                dsw(DV.tensor_tensor(nt2[:, 0:n], out_ap, out_ap, ALU.mult))
                dsw(DV.tensor_tensor(nt2[:, 0:n], nt2[:, 0:n], nt1[:, 0:n], ALU.mult))
                dsw(DV.tensor_scalar(nt2[:, 0:n], nt2[:, 0:n], -0.5, 1.5, ALU.mult, ALU.add))
                dsw(DV.tensor_tensor(out_ap, out_ap, nt2[:, 0:n], ALU.mult))

        def ln_stats(srcs):
            n = len(srcs)
            last = None
            for kk, (h0, h1) in enumerate(srcs):
                DV.bn_stats(stats[:, kk * 12:kk * 12 + 6], h0)
                last = DV.bn_stats(stats[:, kk * 12 + 6:kk * 12 + 12], h1)
            dsw(last)
            for kk in range(n):
                last = DV.bn_aggr(aggr[:, 2 * kk:2 * kk + 2], stats[:, kk * 12:(kk + 1) * 12])
            dsw(last)
            ln_newton(n, aggr[:, 1:2 * n:2], rstd[:, 0:n])

        def slot_of(c, tb):
            return ((c * NTB + tb) % 8) * E

        def xn_compute(c, srcs):
            ln_stats(srcs)
            ms = []
            for tb in range(NTB):
                s_, v_ = state["xnst_free"][tb]
                W(DV, s_, v_)
                r = DV.tensor_scalar(xnst[:, tb * E:(tb + 1) * E],
                                     xr[:, slot_of(c, tb):slot_of(c, tb) + E],
                                     aggr[:, 2 * tb:2 * tb + 1], rstd[:, tb:tb + 1],
                                     ALU.subtract, ALU.mult)
                ms.append(dsw(r))
            return ms

        def xn_transpose_phase(dst, xn_ms):
            te_tb = [0] * NTB
            for bi in range(4):
                bank_acquire(bi)
            for tb in range(NTB):
                W(TEe, "sc", sw_consts)
                W(TEe, "dve", xn_ms[tb])
                mm = None
                for et in range(NET):
                    bi = et % 4
                    pbf = pb[bi][:].bitcast(BF16)
                    mm = TEe.transpose(
                        pbf[:, (et // 4) * 512 + tb * 128:(et // 4) * 512 + (tb + 1) * 128],
                        xnst[:, tb * E + et * 128:tb * E + et * 128 + 128],
                        idb[:])
                te_tb[tb] = I(mm, "te")
            last = 0
            for et in range(NET):
                bi = et % 4
                bdata[bi] = te_tb[NTB - 1]
                bank_read(bi, DV)
                pbf = pb[bi][:].bitcast(BF16)
                r = DV.tensor_copy(dst[:, et * CH:(et + 1) * CH],
                                   pbf[:, (et // 4) * 512:(et // 4) * 512 + 512])
                last = I(r, "dve")
                bfree[bi].append(("dve", last))
            state["xnst_free"] = [("te", te_tb[tb]) for tb in range(NTB)]
            return last

        def ln1_compute(c):
            for tb in range(NTB):
                sname, v = xload[c * NTB + tb]
                W(DV, sname, v)
            return xn_compute(c, [(xr[:, slot_of(c, tb):slot_of(c, tb) + 512],
                                   xr[:, slot_of(c, tb) + 512:slot_of(c, tb) + E])
                                  for tb in range(NTB)])

        def head_phase(c, xn_ms):
            b_, ci = divmod(c, NCH)
            buf = bufA(c)
            if c == 1:
                W(DV, "so0", sw_wohbm[0])
                W(DV, "so1", sw_wohbm[1])
            scr_w = xn_transpose_phase(buf, xn_ms)

            W(TEe, "dve", scr_w)
            qk_ev = []
            for fb in range(16):
                bi = fb % 4
                bank_acquire(bi)
                mm = None
                for et in range(NET):
                    mm = TEe.matmul(pb[bi][:],
                                    wqkv[:, et * 2 * E + fb * 128:et * 2 * E + fb * 128 + 128],
                                    buf[:, et * CH:(et + 1) * CH],
                                    start=(et == 0), stop=(et == NET - 1))
                bdata[bi] = I(mm, "te")
                bank_read(bi, AC)
                if fb < 8:
                    dst = qsb[:, fb * CH:(fb + 1) * CH]
                else:
                    ft = fb - 8
                    dst = ktc[:, ft * T + ci * CH: ft * T + (ci + 1) * CH]
                qk_ev.append(I(AC.copy(dst, pb[bi][:]), "act"))
                bfree[bi].append(("act", qk_ev[-1]))
            for bi in range(8):
                bank_acquire(bi)
            v_mm = [0] * NET
            for et in range(NET):
                sl = et % 2
                W(SP, "so0", sw_vhbm[0])
                W(SP, "so1", sw_vhbm[1])
                W(SP, "te", v_mm[et - 2] if et >= 2 else state["vdb_guard"][sl])
                v_ld = I(SP.dma_start(vdb[:, sl * E:(sl + 1) * E],
                                      vhbm[et * 128:(et + 1) * 128, :]), f"sv{sl}", 16)
                W(TEe, f"sv{sl}", v_ld)
                mm = None
                for tb in range(NTB):
                    for half in range(2):
                        bi = 2 * tb + half
                        mm = TEe.matmul(
                            pb[bi][:],
                            buf[:, et * CH + tb * 128:et * CH + tb * 128 + 128],
                            vdb[:, sl * E + half * 512:sl * E + (half + 1) * 512],
                            start=(et == 0), stop=(et == NET - 1))
                v_mm[et] = I(mm, "te")
            state["vdb_guard"] = [v_mm[NET - 2], v_mm[NET - 1]]
            v_ev = []
            for tb in range(NTB):
                for half in range(2):
                    bi = 2 * tb + half
                    bdata[bi] = v_mm[NET - 1]
                    bank_read(bi, DV)
                    j = 4 * ci + tb
                    r = DV.tensor_copy(vc[:, j * E + half * 512:j * E + (half + 1) * 512],
                                       pb[bi][:])
                    v_ev.append(I(r, "dve"))
                    bfree[bi].append(("dve", v_ev[-1]))
            state["bufA_readers"][c % 2] = cnt["te"]
            state["q_ms"] = max(qk_ev)
            state["v_ms"] = max(v_ev)

        def att_phase(c, do_next):
            b_, ci = divmod(c, NCH)
            njs = 4 * (ci + 1)
            xn_next = None
            W(TEe, "act", state["q_ms"])
            W(TEe, "dve", state["v_ms"])
            sc_ms = {}
            exp_ms = {}

            def emit_scores(ft, j):
                mm = None
                for hh in range(2):
                    bi = 2 * (j % 2) + hh
                    bank_acquire(bi)
                    mm = TEe.matmul(pb[bi][:],
                                    ktc[hh * 64:hh * 64 + 64,
                                        ft * T + j * 128:ft * T + (j + 1) * 128],
                                    qsb[hh * 64:hh * 64 + 64, ft * CH:(ft + 1) * CH],
                                    start=True, stop=True)
                v = I(mm, "te")
                sc_ms[(ft, j)] = v
                bdata[2 * (j % 2)] = v
                bdata[2 * (j % 2) + 1] = v

            def emit_exp(ft, j):
                sl = j % 2
                W(AC, "te", sc_ms[(ft, j)])
                for kk in (2 * sl, 2 * sl + 1):
                    s_, v_ = state["exp_guard"][kk]
                    W(AC, s_, v_)
                AC.activation(expb[:, (2 * sl) * CH:(2 * sl + 1) * CH], pb[2 * (j % 2)][:],
                              AF.Exp, scale=0.125)
                r = AC.activation(expb[:, (2 * sl + 1) * CH:(2 * sl + 2) * CH],
                                  pb[2 * (j % 2) + 1][:], AF.Exp, scale=0.125)
                exp_ms[(ft, j)] = I(r, "act", 2)
                bfree[2 * (j % 2)].append(("act", exp_ms[(ft, j)]))
                bfree[2 * (j % 2) + 1].append(("act", exp_ms[(ft, j)]))

            def emit_pvs(p, ft, j):
                sl = j % 2
                W(TEe, "act", exp_ms[(ft, j)])
                bv = 4 + (p % 2)
                if j == 0:
                    bank_acquire(bv)
                    if p == 0:
                        bank_acquire(6)
                eA = expb[:, (2 * sl) * CH:(2 * sl + 1) * CH]
                eB = expb[:, (2 * sl + 1) * CH:(2 * sl + 2) * CH]
                TEe.matmul(pb[bv][0:64, :],
                           vc[:, j * E + (2 * ft) * 64:j * E + (2 * ft) * 64 + 64],
                           eA, start=(j == 0), stop=(j == njs - 1))
                TEe.matmul(pb[bv][64:128, :],
                           vc[:, j * E + (2 * ft + 1) * 64:j * E + (2 * ft + 1) * 64 + 64],
                           eB, start=(j == 0), stop=(j == njs - 1))
                hg = 2 * p
                TEe.matmul(pb[6][32 * hg:32 * hg + 32, :], on32[:, 0:32], eA,
                           start=(j == 0), stop=(j == njs - 1), tile_position=(0, 32 * hg))
                mm = TEe.matmul(pb[6][32 * hg + 32:32 * hg + 64, :], on32[:, 0:32], eB,
                                start=(j == 0), stop=(j == njs - 1),
                                tile_position=(0, 32 * hg + 32))
                v = I(mm, "te")
                state["exp_guard"][2 * sl] = ("te", v)
                state["exp_guard"][2 * sl + 1] = ("te", v)
                if j == njs - 1:
                    bdata[bv] = v
                    if p == 1:
                        bdata[6] = v

            cons_te = 0
            for g in range(4):
                for p in range(2):
                    ft = 2 * g + p
                    emit_scores(ft, 0)
                    for j in range(njs):
                        if j + 1 < njs:
                            emit_scores(ft, j + 1)
                        emit_exp(ft, j)
                        emit_pvs(p, ft, j)
                    bv = 4 + (p % 2)
                    bank_read(bv, DV)
                    r = DV.tensor_copy(pvraw[:, ft * CH:(ft + 1) * CH], pb[bv][:])
                    bfree[bv].append(("dve", I(r, "dve")))
                gp = g % 2
                bank_read(6, DV)
                s_, v_ = state["shilo_guard"][gp]
                W(DV, s_, v_)
                dsw(DV.tensor_copy(shilo[:, (2 * gp) * CH:(2 * gp + 1) * CH], pb[6][:]))
                r = DV.tensor_tensor(shilo[:, (2 * gp + 1) * CH:(2 * gp + 2) * CH], pb[6][:],
                                     shilo[:, (2 * gp) * CH:(2 * gp + 1) * CH], ALU.subtract)
                sev = I(r, "dve")
                bfree[6].append(("dve", sev))
                if g == 0:
                    bank_acquire(7)
                W(TEe, "dve", sev)
                TEe.matmul(pb[7][0:16, :], indc[:, g * 16:(g + 1) * 16],
                           shilo[:, (2 * gp) * CH:(2 * gp + 1) * CH],
                           start=(g == 0), stop=False)
                mm = TEe.matmul(pb[7][0:16, :], indc[:, g * 16:(g + 1) * 16],
                                shilo[:, (2 * gp + 1) * CH:(2 * gp + 2) * CH],
                                start=False, stop=(g == 3))
                cons_te = I(mm, "te")
                state["shilo_guard"][gp] = ("te", cons_te)
                if g == 0 and do_next:
                    xn_next = ln1_compute(c + 1)
            bdata[7] = cons_te
            bank_read(7, DV)
            for kk in (0, 1):
                s_, v_ = state["exp_guard"][kk]
                W(DV, s_, v_)
            r = DV.tensor_copy(cmpv[0:16, 0:512], pb[7][0:16, :])
            scons_sb = I(r, "dve")
            bfree[7].append(("dve", scons_sb))
            return xn_next

        def fhn_phase(c):
            Fs = lambda kk: fhn[:, kk * CH:(kk + 1) * CH]
            Ssb = cmpv[0:16, 0:512]
            W(DV, "dve", v_fhnc)
            dsw(DV.reciprocal(Fs(0), Ssb))
            dsw(DV.tensor_tensor(Fs(1), Ssb, Fs(0), ALU.mult))
            dsw(DV.tensor_scalar(Fs(2), Fs(1), 1e-6, None, ALU.max))
            dsw(DV.tensor_scalar(Fs(3), Fs(2), -1.0, 2.0, ALU.mult, ALU.add))
            v_sn = dsw(DV.tensor_tensor(Fs(3), Fs(1), Fs(3), ALU.mult))
            W(AC, "dve", v_sn)
            v_t = I(AC.activation(Fs(4), Fs(2), AF.Tanh, scale=5.0, bias=fhnc[:, 3:4]), "act")
            W(DV, "act", v_t)
            dsw(DV.tensor_scalar(Fs(4), Fs(4), 0.45, 0.55, ALU.mult, ALU.add))
            dsw(DV.tensor_tensor(Fs(4), Fs(3), Fs(4), ALU.mult))
            dsw(DV.tensor_scalar(Fs(3), Fs(4), abdt[:, 2:3], None, ALU.mult))
            v_v1 = dsw(DV.tensor_scalar(Fs(3), Fs(3), 3.0, -3.0, ALU.min, ALU.max))
            W(AC, "dve", v_v1)
            v_sq = I(AC.activation(Fs(2), Fs(3), AF.Square), "act")
            dsw(DV.tensor_scalar(Fs(0), Fs(3), abdt[:, 0:1], fhnc[:, 1:2], ALU.add, ALU.mult))
            dsw(DV.tensor_scalar(Fs(0), Fs(0), 3.0, -3.0, ALU.min, ALU.max))
            W(DV, "act", v_sq)
            dsw(DV.scalar_tensor_tensor(Fs(2), Fs(2), 1.0 / 3.0, Fs(3), ALU.mult, ALU.mult))
            dsw(DV.tensor_tensor(Fs(2), Fs(3), Fs(2), ALU.subtract))
            dsw(DV.tensor_tensor(Fs(2), Fs(2), Fs(0), ALU.subtract))
            dsw(DV.tensor_tensor(Fs(2), Fs(2), Fs(4), ALU.add))
            dsw(DV.scalar_tensor_tensor(Fs(2), Fs(2), abdt[:, 2:3], Fs(3), ALU.mult, ALU.add))
            dsw(DV.tensor_scalar(Fs(2), Fs(2), 3.0, -3.0, ALU.min, ALU.max))
            dsw(DV.tensor_scalar(Fs(0), Fs(1), 1e-6, None, ALU.max))
            v_resp = dsw(DV.tensor_tensor(Fs(2), Fs(2), Fs(0), ALU.mult))
            W(AC, "dve", v_resp)
            v_t2 = I(AC.activation(Fs(2), Fs(2), AF.Tanh, scale=0.5), "act")
            W(DV, "act", v_t2)
            dsw(DV.tensor_scalar(Fs(2), Fs(2), 0.25, 0.75, ALU.mult, ALU.add))
            dsw(DV.tensor_tensor(Fs(3), Fs(2), Fs(1), ALU.mult))
            dsw(DV.tensor_scalar(Fs(3), Fs(3), 1e-8, None, ALU.add))
            v_D = dsw(DV.tensor_tensor(Fs(3), Fs(3), Ssb, ALU.mult))
            dsw(DV.reciprocal(Fs(3), Fs(3)))
            dsw(DV.tensor_tensor(Fs(3), Fs(2), Fs(3), ALU.mult))
            s_, v_ = state["mhilo_guard"]
            W(DV, s_, v_)
            dsw(DV.tensor_copy(mhilo[0:16, :], Fs(3)))
            r = DV.tensor_tensor(mhilo[32:48, :], Fs(3), mhilo[0:16, :], ALU.subtract)
            m_ms = I(r, "dve")
            state["exp_guard"][0] = ("dve", v_D)
            state["exp_guard"][1] = ("dve", v_D)
            return m_ms

        def mb_phase(c, m_ms):
            buf = bufA(c)
            at_ms = [0] * NFT
            for ft in range(NFT):
                bi = ft % 2
                bank_acquire(bi)
                W(TEe, "dve", m_ms)
                mm = TEe.matmul(pb[bi][:], indm[0:64, ft * 128:(ft + 1) * 128],
                                mhilo[0:64, :], start=True, stop=True)
                bdata[bi] = I(mm, "te")
                bank_read(bi, DV)
                W(DV, "te", state["bufA_readers"][c % 2])
                r = DV.tensor_tensor(buf[:, ft * CH:(ft + 1) * CH],
                                     pvraw[:, ft * CH:(ft + 1) * CH], pb[bi][:], ALU.mult)
                at_ms[ft] = I(r, "dve")
                bfree[bi].append(("dve", at_ms[ft]))
            state["mhilo_guard"] = ("te", cnt["te"])
            return max(at_ms)

        def op_phase(c, at_ms):
            buf = bufA(c)
            for bi in range(8):
                bank_acquire(bi)
            W(TEe, "dve", at_ms)
            w_mm = [0] * NFT
            for ft in range(NFT):
                sl = ft % 2
                W(SP, "so0", sw_wohbm[0])
                W(SP, "so1", sw_wohbm[1])
                W(SP, "te", w_mm[ft - 2] if ft >= 2 else state["wdb_guard"][sl])
                w_ld = I(SP.dma_start(wdb[:, sl * E:(sl + 1) * E],
                                      wohbm[ft * 128:(ft + 1) * 128, :]), f"sw{sl}", 16)
                W(TEe, f"sw{sl}", w_ld)
                mm = None
                for tb in range(NTB):
                    for half in range(2):
                        bi = 2 * tb + half
                        mm = TEe.matmul(
                            pb[bi][:],
                            buf[:, ft * CH + tb * 128:ft * CH + tb * 128 + 128],
                            wdb[:, sl * E + half * 512:sl * E + (half + 1) * 512],
                            start=(ft == 0), stop=(ft == NFT - 1))
                w_mm[ft] = I(mm, "te")
            state["wdb_guard"] = [w_mm[NFT - 2], w_mm[NFT - 1]]
            for bi in range(8):
                bdata[bi] = w_mm[NFT - 1]

        def lna_phase(c):
            for tb in range(NTB):
                bank_read(2 * tb, DV)
                bank_read(2 * tb + 1, DV)
            ln_stats([(pb[2 * tb][:], pb[2 * tb + 1][:]) for tb in range(NTB)])
            for tb in range(NTB):
                for half in range(2):
                    r = DV.tensor_scalar(tmp[:, half * 512:(half + 1) * 512],
                                         pb[2 * tb + half][:],
                                         aggr[:, 2 * tb:2 * tb + 1], rstd[:, tb:tb + 1],
                                         ALU.subtract, ALU.mult)
                dsw(r)
                r = DV.tensor_tensor(xr[:, slot_of(c, tb):slot_of(c, tb) + E],
                                     xr[:, slot_of(c, tb):slot_of(c, tb) + E],
                                     tmp[:, 0:E], ALU.add)
                v = dsw(r)
                bfree[2 * tb].append(("dve", v))
                bfree[2 * tb + 1].append(("dve", v))

        def ln2t2_phase(c):
            xn2_ms = xn_compute(c, [(xr[:, slot_of(c, tb):slot_of(c, tb) + 512],
                                     xr[:, slot_of(c, tb) + 512:slot_of(c, tb) + E])
                                    for tb in range(NTB)])
            return xn_transpose_phase(pvraw, xn2_ms)

        def ffn_phase(c, scr_w2):
            W(TEe, "dve", max(v_tern, scr_w2))
            for bi in range(8):
                bank_acquire(bi)
            mm = None
            for et in range(NET):
                for tb in range(NTB):
                    for half in range(2):
                        bi = 2 * tb + half
                        mm = TEe.matmul(
                            pb[bi][:],
                            pvraw[:, et * CH + tb * 128:et * CH + tb * 128 + 128],
                            tern[:, et * E + half * 512:et * E + (half + 1) * 512],
                            start=(et == 0), stop=(et == NET - 1))
            ff_te = I(mm, "te")
            for bi in range(8):
                bdata[bi] = ff_te

        def out3_slot(c, tb):
            if tb == 0:
                return tmp[:, 0:E]
            if tb <= 2:
                bv = bufA(c)[:].bitcast(F32)
                return bv[:, (tb - 1) * E:tb * E]
            return cmpv[:, 0:E]

        def ln3_store_phase(c):
            b_, ci = divmod(c, NCH)
            W(DV, "sc", sw_consts)
            for tb in range(NTB):
                o3 = out3_slot(c, tb)
                if tb == 3:
                    for kk in range(4):
                        s_, v_ = state["exp_guard"][kk]
                        W(DV, s_, v_)
                for half in range(2):
                    bi = 2 * tb + half
                    bank_read(bi, DV)
                    r = DV.scalar_tensor_tensor(o3[:, half * 512:(half + 1) * 512],
                                                pb[bi][:], alphar[:, 0:1],
                                                biasr[:, half * 512:(half + 1) * 512],
                                                ALU.mult, ALU.add)
                    bfree[bi].append(("dve", dsw(r)))
            ln_stats([(out3_slot(c, tb)[:, 0:512], out3_slot(c, tb)[:, 512:E])
                      for tb in range(NTB)])
            for tb in range(NTB):
                gtb = c * NTB + tb
                o3 = out3_slot(c, tb)
                dsw(DV.tensor_scalar(o3[:, 0:E], o3[:, 0:E],
                                     aggr[:, 2 * tb:2 * tb + 1], rstd[:, tb:tb + 1],
                                     ALU.subtract, ALU.mult))
                r = DV.tensor_tensor(xr[:, slot_of(c, tb):slot_of(c, tb) + E],
                                     xr[:, slot_of(c, tb):slot_of(c, tb) + E],
                                     o3[:, 0:E], ALU.add)
                y2_ms[gtb] = dsw(r)
                if tb == 3:
                    for kk in range(4):
                        state["exp_guard"][kk] = ("dve", y2_ms[gtb])
                W(SP, "dve", y2_ms[gtb])
                d = SP.dma_start(y_h[b_, (ci * CH + tb * 128):(ci * CH + tb * 128 + 128), :],
                                 xr[:, slot_of(c, tb):slot_of(c, tb) + E])
                I(d, f"xr{gtb % 8}", 16)

        nchunks = nb * NCH
        head_phase(0, ln1_compute(0))
        for c in range(nchunks):
            if c + 1 < nchunks:
                b2, c2 = divmod(c + 1, NCH)
                for tb in range(NTB):
                    emit_xload(b2, c2, tb)
            xn_next = att_phase(c, c + 1 < nchunks)
            if c + 1 < nchunks:
                head_phase(c + 1, xn_next)
            m_ms = fhn_phase(c)
            at_ms = mb_phase(c, m_ms)
            op_phase(c, at_ms)
            lna_phase(c)
            scr_w2 = ln2t2_phase(c)
            ffn_phase(c, scr_w2)
            ln3_store_phase(c)

        if dbg:
            W(SP, "dve", cnt["dve"])
            W(SP, "act", cnt["act"])
            W(SP, "te", cnt["te"])
            named = {"ktc": ktc, "vc": vc, "qsb": qsb, "pvraw": pvraw, "scrA": scrA,
                     "scrB": scrB, "shilo": shilo, "mhilo": mhilo, "tern": tern,
                     "xr": xr, "expb": expb, "delta": delta, "aggr": aggr,
                     "rstd": rstd, "fhn": fhn, "fhnc": fhnc, "tmp": tmp, "dsc": dsc,
                     "wqkv": wqkv, "vdb": vdb, "wdb": wdb}
            for n, hh in dbg_h.items():
                I(SP.dma_start(hh, named[n][:]), "sc", 16)
        for i2 in range(8):
            W(SP, f"xr{i2}", cnt[f"xr{i2}"])
        W(SP, "sc", cnt["sc"])

    return nc


def _make_consts():
    ident = np.eye(128, dtype=np.float32)
    ones32 = np.ones((128, 32), np.float32)
    allones = np.ones((128, 128), np.float32)
    indcons = np.zeros((128, 4 * 16), np.float32)
    for g in range(4):
        for m in range(16):
            if 4 * g <= m < 4 * g + 4:
                indcons[32 * (m - 4 * g), g * 16 + m] = 1.0
    indmb = np.zeros((64, 8 * 128), np.float32)
    for ft in range(8):
        for p in range(128):
            h = 2 * ft + (1 if p >= 64 else 0)
            indmb[h, ft * 128 + p] = 1.0
            indmb[32 + h, ft * 128 + p] = 1.0
    return ident, ones32, allones, indcons, indmb


def kernel(L0_attn_x, L0_attn_spectral_basis, L0_attn_W_qkv, L0_attn_W_out,
           L0_attn_a, L0_attn_b, L0_attn_dt, L0_ffn_weight, L0_ffn_alpha,
           L0_ffn_bias):
    x = np.ascontiguousarray(np.asarray(L0_attn_x, dtype=np.float32))
    wqkvT = np.ascontiguousarray(np.asarray(L0_attn_W_qkv, np.float32).T)
    woutT = np.ascontiguousarray(np.asarray(L0_attn_W_out, np.float32).T)
    wffnT = np.ascontiguousarray(np.asarray(L0_ffn_weight, np.float32).T)
    bias = np.asarray(L0_ffn_bias, np.float32).reshape(1, E)
    biasrep = np.ascontiguousarray(np.broadcast_to(bias, (128, E)))
    alpharep = np.ascontiguousarray(
        np.broadcast_to(np.asarray(L0_ffn_alpha, np.float32).reshape(1, 1), (128, 1)))
    abdt = np.ascontiguousarray(np.broadcast_to(
        np.stack([np.asarray(L0_attn_a, np.float32).reshape(()),
                  np.asarray(L0_attn_b, np.float32).reshape(()),
                  np.asarray(L0_attn_dt, np.float32).reshape(())]).reshape(1, 3), (16, 3)))
    ident, ones32, allones, indcons, indmb = [
        a.astype(ml_dtypes.bfloat16) for a in _make_consts()]

    nc = build_program(BPC)
    in_maps = []
    for c in range(NCORES):
        in_maps.append({
            "x": np.ascontiguousarray(x[c * BPC:(c + 1) * BPC]),
            "wqkvT": wqkvT, "woutT": woutT, "wffnT": wffnT,
            "biasrep": biasrep, "alpharep": alpharep, "abdt": abdt,
            "identc": ident, "ones32c": ones32, "allonesc": allones,
            "indconsc": indcons, "indmbc": indmb,
        })
    res = run_bass_kernel_spmd(nc, in_maps, core_ids=list(range(NCORES)))
    y = np.concatenate([np.asarray(r["y"]) for r in res.results], axis=0)
    return y.astype(np.float32)
